# version 01
# baseline (speedup 1.0000x reference)
import numpy as np
import ml_dtypes
import concourse.bass as bass
import concourse.mybir as mybir

F32 = mybir.dt.float32
BF16 = mybir.dt.bfloat16
I32 = mybir.dt.int32
AF = mybir.ActivationFunctionType
ALU = mybir.AluOpType
bf = ml_dtypes.bfloat16
f32 = np.float32
n32 = np.float32

NCORES = 8
TOK = 8192
D = 2048
TPC = 1024
H = 16
BLK = 256
BPC = 4
MLP = 16384
MLPC = 2048
NCH = 16
GBIG = 4.0e4
LN_EPS = 1e-5
RMS_EPS = 1e-6
SQ_A = float(np.sqrt(0.044715))
GELU_B = 0.7978845608028654

RX, RXN, RSQ, RWS, RKD = 0, 16384, 24576, 32768, 40960


def f_perm(n):
    return np.arange(n).reshape(n // 256, 128, 2).transpose(0, 2, 1).reshape(-1)


F_OF_PHI = f_perm(D)
PI16 = f_perm(MLP)


def _p2(C):
    blk, vf, vb, vi, PB, PBb = C.blk, C.vf, C.vb, C.vi, C.PB, C.PBb
    inc, wait, M = C.inc, C.wait, C.M
    dmx = C.dmx

    stb = lambda par: vb(RSQ + par * 512, 1024)
    stu = vf(RSQ + 1024, 1024)
    vt_st = lambda par: vb(RSQ + 2048 + par * 256, 512)
    kdm_st = lambda par: vb(RSQ + 2560 + par * 256, 512)
    c_st = lambda i: vf(RSQ + 3072 + i * 128, 128)
    e_sl = vf(RSQ + 3584, 2048)
    kd_dec = vb(RSQ + 5632, 4096)
    kdT = vb(RKD, 16384)
    xnb = vb(RXN, 16384)
    wsl = lambda par: vb(RWS + par * 4096, 8192)

    dec_emitters = _decode_emitters(C)

    def pre_a(s):
        wait(s, "A", M["p0"], M["nslc"], M["xn_v_1"])
        m = None
        for h in range(H):
            m = inc(s.activation(kd_dec[:, h * BLK:(h + 1) * BLK], C.sb_rev[:, :],
                                 AF.Exp, scale=C.sb_nslc[:, h:h + 1]), "a")
        M["kd_dec"] = m
    blk.scalar(pre_a)

    def pre_v(v):
        wait(v, "V", M["xn_v_1"])
        M["e_zero"] = inc(v.memset(e_sl, 0.0), "v")
    blk.vector(pre_v)

    def pre_s(s):
        wait(s, "S", M["wcast"])
        M["slab_0"] = dmx(s, "sA", wsl(0),
                          C.wq_bf[0].rearrange("p kc j -> p (kc j)"))
    blk.sync(pre_s)

    for g in range(16):
        def p2_ld(s, g=g):
            if g + 1 < 16:
                if g >= 1:
                    wait(s, "S", M[f"p2mmg_{g - 1}"])
                M[f"slab_{g + 1}"] = dmx(
                    s, "sA", wsl((g + 1) % 2),
                    C.wq_bf[g + 1].rearrange("p kc j -> p (kc j)"))
        blk.sync(p2_ld)

        for ol in range(4):
            oc = 4 * g + ol

            def p2_mm(t, g=g, ol=ol, oc=oc):
                wait(t, "T", M[f"slab_{g}"], M["xn_v_1"])
                for tt in range(2):
                    bank = PB[2 * (ol % 2) + tt]
                    if oc >= 2:
                        wait(t, "T", M[f"ev_{oc - 2}_{tt}"])
                    mm = None
                    for kc in range(NCH):
                        mm = t.matmul(
                            bank[:, :],
                            wsl(g % 2)[:, kc * 512 + ol * 128: kc * 512 + ol * 128 + 128],
                            xnb[:, kc * 1024 + tt * 512: kc * 1024 + tt * 512 + 512],
                            start=(kc == 0), stop=(kc == NCH - 1))
                    M[f"mm_{oc}_{tt}"] = inc(mm, "t")
                if ol == 3:
                    M[f"p2mmg_{g}"] = ("t", C.cnt["t"])
            blk.tensor(p2_mm)

            def p2_ev(s, oc=oc, ol=ol):
                for tt in range(2):
                    wait(s, "A", M[f"mm_{oc}_{tt}"])
                    prev = oc - 2
                    if prev >= 0 and oc < 48:
                        wait(s, "A", M.get(f"qst_{prev}"))
                        if 16 <= prev < 32:
                            wait(s, "A", M.get(f"kdm_{prev}_1"))
                        if 32 <= prev < 48:
                            wait(s, "A", M.get(f"vtr_{prev}_1"))
                    bank = PB[2 * (ol % 2) + tt]
                    if oc < 32:
                        dst = stb(oc % 2)[:, tt * 512: tt * 512 + 512]
                        m = inc(s.activation(dst, bank[:, :], AF.Silu), "a")
                    elif oc < 48:
                        dst = stb(oc % 2)[:, tt * 512: tt * 512 + 512]
                        m = inc(s.activation(dst, bank[:, :], AF.Copy), "a")
                    else:
                        if oc >= 49:
                            wait(s, "A", M.get(f"qst_{oc - 1}"))
                        dst = stu[:, tt * 512: tt * 512 + 512]
                        m = inc(s.activation(dst, bank[:, :], AF.Copy), "a")
                    M[f"ev_{oc}_{tt}"] = m
            blk.scalar(p2_ev)

            if oc < 16:
                def p2_stq(s, oc=oc):
                    wait(s, "S", M[f"ev_{oc}_1"])
                    M[f"qst_{oc}"] = dmx(
                        s, "sF",
                        C.q_buf[:, :, oc * BLK:(oc + 1) * BLK].rearrange("b p t -> p b t"),
                        stb(oc % 2).rearrange("p (b t) -> p b t", b=4))
                blk.sync(p2_stq)

            elif oc < 32:
                h = oc - 16

                def p2_stk(s, oc=oc, h=h):
                    wait(s, "S", M[f"ev_{oc}_1"])
                    M[f"qst_{oc}"] = dmx(
                        s, "sF",
                        C.k_buf[:, :, h * BLK:(h + 1) * BLK].rearrange("b p t -> p b t"),
                        stb(oc % 2).rearrange("p (b t) -> p b t", b=4))
                blk.sync(p2_stk)

                for tt in range(2):
                    def p2_kd(v, oc=oc, h=h, tt=tt):
                        wait(v, "V", M[f"ev_{oc}_{tt}"], M["kd_dec"])
                        if oc >= 17:
                            wait(v, "V", M[f"ktr_{oc - 1}_{tt}"])
                        m = None
                        for b_l in range(2):
                            m = inc(v.tensor_mul(
                                kdm_st(tt)[:, b_l * BLK: b_l * BLK + BLK],
                                stb(oc % 2)[:, tt * 512 + b_l * BLK: tt * 512 + b_l * BLK + BLK],
                                kd_dec[:, h * BLK:(h + 1) * BLK]), "v")
                        M[f"kdm_{oc}_{tt}"] = m
                    blk.vector(p2_kd)

                    def p2_ktr(t, oc=oc, h=h, tt=tt):
                        wait(t, "T", M[f"kdm_{oc}_{tt}"])
                        prev = (oc - 2, 1) if tt == 0 else (oc, 0)
                        if f"kdTe_{prev[0]}_{prev[1]}" in M:
                            wait(t, "T", M[f"kdTe_{prev[0]}_{prev[1]}"])
                        mm = None
                        for j in range(4):
                            b_l, half = j // 2, j % 2
                            mm = t.transpose(
                                PBb[4][:, j * 128:(j + 1) * 128],
                                kdm_st(tt)[:, b_l * BLK + half * 128: b_l * BLK + half * 128 + 128],
                                C.sb_id[:, :])
                        M[f"ktr_{oc}_{tt}"] = inc(mm, "t")
                    blk.tensor(p2_ktr)

                    def p2_kev(v, oc=oc, h=h, tt=tt):
                        wait(v, "V", M[f"ktr_{oc}_{tt}"])
                        m = inc(v.tensor_copy(
                            kdT[:, h * 1024 + tt * 512: h * 1024 + tt * 512 + 512],
                            PBb[4][:, 0:512]), "v")
                        M[f"kdTe_{oc}_{tt}"] = m
                        if tt == 1:
                            M[f"kdT_done_{oc}"] = m
                    blk.vector(p2_kev)

            elif oc < 48:
                h = oc - 32

                for tt in range(2):
                    def p2_vtr(t, oc=oc, h=h, tt=tt):
                        wait(t, "T", M[f"ev_{oc}_{tt}"])
                        prev = (oc - 2, 1) if tt == 0 else (oc, 0)
                        if f"vte_{prev[0]}_{prev[1]}" in M:
                            wait(t, "T", M[f"vte_{prev[0]}_{prev[1]}"])
                        mm = None
                        for j in range(4):
                            b_l, half = j // 2, j % 2
                            mm = t.transpose(
                                PBb[5][:, j * 128:(j + 1) * 128],
                                stb(oc % 2)[:, tt * 512 + b_l * BLK + half * 128:
                                            tt * 512 + b_l * BLK + half * 128 + 128],
                                C.sb_id[:, :])
                        M[f"vtr_{oc}_{tt}"] = inc(mm, "t")
                    blk.tensor(p2_vtr)

                    def p2_vev(v, oc=oc, h=h, tt=tt):
                        wait(v, "V", M[f"vtr_{oc}_{tt}"])
                        if oc >= 33:
                            wait(v, "V", M[f"vst_{oc - 1}_{tt}"],
                                 M[f"cmm_{oc - 1}_{tt}"])
                        m = inc(v.tensor_copy(vt_st(tt), PBb[5][:, 0:512]), "v")
                        M[f"vte_{oc}_{tt}"] = m
                    blk.vector(p2_vev)

                    def p2_vst(s, oc=oc, h=h, tt=tt):
                        wait(s, "S", M[f"vte_{oc}_{tt}"])
                        M[f"vst_{oc}_{tt}"] = dmx(
                            s, "sG",
                            C.vt_buf[:, :, h * BLK:(h + 1) * BLK].rearrange(
                                "b p t -> p b t")[:, 2 * tt:2 * tt + 2, :],
                            vt_st(tt).rearrange("p (b t) -> p b t", b=2))
                        if tt == 1:
                            M[f"qst_{oc}"] = M[f"vst_{oc}_1"]
                    blk.sync(p2_vst)

                    def p2_cmm(t, oc=oc, h=h, tt=tt):
                        wait(t, "T", M[f"vte_{oc}_{tt}"], M[f"kdT_done_{16 + h}"])
                        prev = (oc - 2, 1) if tt == 0 else (oc, 0)
                        if f"ce_{prev[0]}_{prev[1]}" in M:
                            wait(t, "T", M[f"ce_{prev[0]}_{prev[1]}"])
                        mm = None
                        for b_l in range(2):
                            b = 2 * tt + b_l
                            for half in range(2):
                                mm = t.matmul(
                                    PB[6][:, b_l * 128:(b_l + 1) * 128],
                                    kdT[:, h * 1024 + b * BLK + half * 128:
                                        h * 1024 + b * BLK + half * 128 + 128],
                                    vt_st(tt)[:, b_l * BLK + half * 128:
                                              b_l * BLK + half * 128 + 128],
                                    start=(half == 0), stop=(half == 1))
                        M[f"cmm_{oc}_{tt}"] = inc(mm, "t")
                    blk.tensor(p2_cmm)

                    def p2_ce(v, oc=oc, h=h, tt=tt):
                        wait(v, "V", M[f"cmm_{oc}_{tt}"], M["e_zero"])
                        if tt == 0 and oc >= 33:
                            wait(v, "V", M[f"cst_{oc - 1}"])
                        for b_l in range(2):
                            b = 2 * tt + b_l
                            cs = c_st(b)
                            a = inc(v.tensor_copy(cs, PB[6][:, b_l * 128:(b_l + 1) * 128]), "v")
                            wait(v, "V", a)
                            eh = e_sl[:, h * 128:(h + 1) * 128]
                            if b > 0:
                                a = inc(v.tensor_scalar(eh, eh, C.sb_bdc[:, h:h + 1],
                                                        None, ALU.mult), "v")
                                wait(v, "V", a)
                                a = inc(v.tensor_add(eh, eh, cs), "v")
                            else:
                                a = inc(v.tensor_copy(eh, cs), "v")
                            wait(v, "V", a)
                        M[f"ce_{oc}_{tt}"] = ("v", C.cnt["v"])
                    blk.vector(p2_ce)

                def p2_cst(s, oc=oc, h=h):
                    wait(s, "S", M[f"ce_{oc}_1"])
                    M[f"cst_{oc}"] = dmx(
                        s, "sF",
                        C.c_bufd[:, :, h * 128:(h + 1) * 128].rearrange("b p e -> p b e"),
                        vf(RSQ + 3072, 512).rearrange("p (b e) -> p b e", b=4))
                blk.sync(p2_cst)

            else:
                h = oc - 48

                def p2_stu(s, oc=oc, h=h):
                    wait(s, "S", M[f"ev_{oc}_1"])
                    M[f"qst_{oc}"] = dmx(
                        s, "sF",
                        C.u_buf[:, :, h * BLK:(h + 1) * BLK].rearrange("b p t -> p b t"),
                        stu.rearrange("p (b t) -> p b t", b=4))
                blk.sync(p2_stu)

        for _ in range(3):
            if dec_emitters:
                dec_emitters.pop(0)()

    while dec_emitters:
        dec_emitters.pop(0)()
    M["dec_all_v"] = ("v", C.cnt["v"])
    M["dec_all_p"] = ("p", C.cnt["p"])
    M["dec_all_a"] = ("a", C.cnt["a"])

    def p2_est(s):
        wait(s, "S", M["ce_47_1"])
        M["e_st"] = dmx(s, "sG", C.e_loc, e_sl)
    blk.sync(p2_est)

    def p2_eag(g):
        wait(g, "G", M["e_st"])
        M["ag_e"] = inc(g.collective_compute(
            "AllGather", ALU.bypass, replica_groups=[list(range(NCORES))],
            ins=[C.e_loc.opt()], outs=[C.e_all.opt()]), "c2")
    blk.gpsimd(p2_eag)

    M["p2_pe_end"] = ("t", C.cnt["t"])


def _decode_emitters(C):
    blk, vf, vb, vi, M, inc, wait = C.blk, C.vf, C.vb, C.vi, C.M, C.inc, C.wait
    dmx = C.dmx

    def offs(setn):
        o = RX + setn * 8192
        return dict(src=o, li=o + 1024, hi=o + 2048, sc=o + 3072,
                    lb=o + 4096, hb=o + 4608, sgn=o + 5120, mp=o + 5632,
                    t1=o + 6144, t2=o + 6656, t3=o + 7168, oA=o + 7680)

    mats = [("gate", C.pk_in["gate"], C.sc_in["gate"], "g"),
            ("down", C.pk_in["down"], C.sc_in["down"], "d"),
            ("up", C.pk_in["up"], C.sc_in["up"], "u")]
    tiles = [(mn, pk, sc, tag, c, j)
             for (mn, pk, sc, tag) in mats for c in range(8) for j in range(2)]

    out = []
    for idx, (mn, pk, sc, tag, c, j) in enumerate(tiles):
        def emit(idx=idx, pk=pk, sc=sc, tag=tag, c=c, j=j):
            o = offs(idx % 2)
            lb, hb = vb(o["lb"], 1024), vb(o["hb"], 1024)
            sgn, mp = vb(o["sgn"], 1024), vb(o["mp"], 1024)
            t1, t2, t3 = vb(o["t1"], 1024), vb(o["t2"], 1024), vb(o["t3"], 1024)
            oA = vb(o["oA"], 1024)
            sct = vf(o["sc"], 1024)

            def d_ld(s):
                if idx < 2:
                    wait(s, "S", M["xn_v_1"])
                else:
                    wait(s, "S", M[f"dec_p_{idx - 2}"], M[f"dec_n_{idx - 2}_1"])
                dmx(s, "sC", vi(o["src"], 1024),
                    pk[c * 128:(c + 1) * 128, j * 1024:(j + 1) * 1024])
                mS = dmx(s, "sD", vf(o["sc"], 1024),
                         sc[c * 128:(c + 1) * 128, j * 1024:(j + 1) * 1024])
                M[f"dec_ld_{idx}"] = ("sC", C.cnt["sC"])
                M[f"dec_lds_{idx}"] = mS
            blk.sync(d_ld)

            def d_cast(v):
                wait(v, "V", M[f"dec_ld_{idx}"])
                if idx >= 2:
                    wait(v, "V", M[f"dec_n_{idx - 2}_1"])
                a = inc(v.tensor_scalar(vi(o["li"], 1024), vi(o["src"], 1024), 15, None,
                                        ALU.bitwise_and), "v")
                b = inc(v.tensor_scalar(vi(o["hi"], 1024), vi(o["src"], 1024), 4, None,
                                        ALU.logical_shift_right), "v")
                M[f"dec_p_{idx}"] = b
                wait(v, "V", b)
                inc(v.tensor_copy(lb, vi(o["li"], 1024)), "v")
                m = inc(v.tensor_copy(hb, vi(o["hi"], 1024)), "v")
                M[f"dec_cast_{idx}"] = m
            blk.vector(d_cast)

            for nib in range(2):
                x = lb if nib == 0 else hb

                def d_sgn(s, x=x, nib=nib):
                    wait(s, "A", M[f"dec_cast_{idx}"])
                    if nib == 1:
                        wait(s, "A", M[f"dec_n_{idx}_0"])
                    m = inc(s.activation(sgn, x, AF.Sign, bias=n32(7.5),
                                         scale=n32(-1.0)), "a")
                    M[f"dec_s_{idx}_{nib}"] = m
                blk.scalar(d_sgn)

                def d_chain(v, x=x, nib=nib):
                    wait(v, "V", M[f"dec_s_{idx}_{nib}"])
                    a = inc(v.scalar_tensor_tensor(mp, sgn, 4.0, x, ALU.mult, ALU.add), "v")
                    wait(v, "V", a)
                    inc(v.tensor_scalar(t1, mp, 0.5, -2.0, ALU.mult, ALU.add), "v")
                    inc(v.tensor_scalar(t2, mp, 1.0, -6.0, ALU.mult, ALU.add), "v")
                    a = inc(v.tensor_scalar(t3, mp, 3.0, -27.0, ALU.mult, ALU.add), "v")
                    wait(v, "V", a)
                    a = inc(v.tensor_max(t1, t1, t2), "v")
                    wait(v, "V", a)
                    a = inc(v.tensor_max(t1, t1, t3), "v")
                    wait(v, "V", a)
                    a = inc(v.tensor_mul(t2, t1, sgn), "v")
                    wait(v, "V", a)
                    if nib == 1:
                        wait(v, "V", M[f"dec_stA_{idx}"])
                    wait(v, "V", M[f"dec_lds_{idx}"])
                    a = inc(v.tensor_mul(oA, t2, sct), "v")
                    M[f"dec_n_{idx}_{nib}"] = a
                blk.vector(d_chain)

                def d_st(s, nib=nib):
                    wait(s, "S", M[f"dec_n_{idx}_{nib}"])
                    r = 2 * c + nib
                    if tag == "g":
                        m = dmx(s, "sE",
                                C.dec_g[r * 128:(r + 1) * 128, j * 1024:(j + 1) * 1024],
                                oA)
                    elif tag == "d":
                        m = dmx(s, "sE",
                                C.dec_d[r * 128:(r + 1) * 128, j * 1024:(j + 1) * 1024],
                                oA)
                    else:
                        m = dmx(s, "sE",
                                C.dec_u[8 * j:8 * j + 8, :, r * 128:(r + 1) * 128
                                        ].rearrange("n p t -> p n t"),
                                oA.rearrange("p (n t) -> p n t", n=8))
                    key = f"dec_stA_{idx}" if nib == 0 else f"dec_stB_{idx}"
                    M[key] = m
                    if nib == 1:
                        M[f"dec_v_{idx}"] = M[f"dec_stB_{idx}"]
                        if idx == 15:
                            M["dec_g_done"] = M[key]
                        if idx == 31:
                            M["dec_d_done"] = M[key]
                        if idx == 47:
                            M["dec_u_done"] = M[key]
                blk.sync(d_st)
        out.append(emit)
    return out


def _p3(C):
    blk, vf, vb, PB, PBb = C.blk, C.vf, C.vb, C.PB, C.PBb
    inc, wait, M = C.inc, C.wait, C.M
    dmx = C.dmx

    gres = vb(RX, 32768)
    dres = vb(RXN, 32768)
    ring = lambda r: vb(RWS + r * 4096, 8192)
    inter = vb(RKD, 8192)
    upsl = lambda par: vb(RKD + 4096 + par * 1024, 2048)
    stgD = lambda par: vf(RKD + 6144 + par * 512, 512)
    g0 = vb(RKD + 7168, 512)
    gA = vb(RKD + 7424, 512)
    gB = vb(RKD + 7680, 512)
    gC = vb(RKD + 7936, 512)

    def p3_pre(s):
        wait(s, "S", M["dec_g_done"], M["dec_d_done"], M["dec_u_done"],
             M["p2_pe_end"], M["dec_all_v"], M["dec_all_p"], M["dec_all_a"],
             M["qst_63"], M["e_st"])
        for dc in range(16):
            dmx(s, "sH", gres[:, dc * 2048:(dc + 1) * 2048],
                C.dec_g[dc * 128:(dc + 1) * 128, :])
        for ncD in range(16):
            dmx(s, "sH", dres[:, ncD * 2048:(ncD + 1) * 2048],
                C.dec_d[ncD * 128:(ncD + 1) * 128, :])
        M["p3_res"] = ("sH", C.cnt["sH"])
    blk.sync(p3_pre)

    def ring_load(s, it):
        sl, rq = it // 4, it % 4
        if it < 2:
            wait(s, "S", ("c1", 1), M["p2mmg_15"])
        else:
            wait(s, "S", M[f"p3mm_{it - 2}"])
        m = None
        for half in range(2):
            r = 2 * rq + half
            m = dmx(s, "sA",
                    ring(it % 2).rearrange("p (kc h t) -> p kc h t", kc=16, h=2)[
                        :, :, half, :],
                    C.xn_full.rearrange("(r c p) t -> r p c t", r=NCORES, p=128)[
                        r, :, :, sl * BLK:(sl + 1) * BLK])
        M[f"ring_{it}"] = m

    def up_load(s, gidx):
        it, nc = gidx // 16, gidx % 16
        if gidx < 2:
            wait(s, "S", M["dec_u_done"], M["p2_pe_end"])
        else:
            pit, pnc = (gidx - 2) // 16, (gidx - 2) % 16
            wait(s, "S", M[f"p3up_{pit}_{pnc}"])
        M[f"upld_{it}_{nc}"] = dmx(s, "sB", upsl(gidx % 2), C.dec_u[nc])

    def p3_pre2(s):
        ring_load(s, 0)
        ring_load(s, 1)
        up_load(s, 0)
        up_load(s, 1)
    blk.sync(p3_pre2)

    for it in range(16):
        sl, rq = it // 4, it % 4

        for nc in range(16):
            gidx = it * 16 + nc

            def p3_mm(t, it=it, nc=nc, gidx=gidx):
                wait(t, "T", M[f"ring_{it}"], M["p3_res"], M[f"upld_{it}_{nc}"])
                bg = PB[nc % 2]
                bu = PB[2 + nc % 2]
                pg = gidx - 2
                if pg >= 0:
                    pit, pnc = pg // 16, pg % 16
                    wait(t, "T", M[f"e_s0_{pit}_{pnc}"], M[f"e_s1_{pit}_{pnc}"],
                         M[f"e_up_{pit}_{pnc}"])
                for kc in range(16):
                    t.matmul(bg[:, :],
                             gres[:, kc * 2048 + nc * 128: kc * 2048 + nc * 128 + 128],
                             ring(it % 2)[:, kc * 512:(kc + 1) * 512],
                             start=(kc == 0), stop=(kc == 15))
                mm = None
                for kc in range(16):
                    mm = t.matmul(bu[:, :], upsl(gidx % 2)[:, kc * 128:(kc + 1) * 128],
                                  ring(it % 2)[:, kc * 512:(kc + 1) * 512],
                                  start=(kc == 0), stop=(kc == 15))
                M[f"p3g_{it}_{nc}"] = inc(mm, "t")
                M[f"p3up_{it}_{nc}"] = M[f"p3g_{it}_{nc}"]
            blk.tensor(p3_mm)

            def p3_upld(s, gidx=gidx):
                if gidx + 2 < 256:
                    up_load(s, gidx + 2)
            blk.sync(p3_upld)

            prev = (it, nc - 1) if nc >= 1 else (it - 1, 15)
            pmark = M.get(f"p3e_{prev[0]}_{prev[1]}") if gidx >= 1 else None
            pw = M.get(f"e_w_{prev[0]}_{prev[1]}") if gidx >= 1 else None

            def p3_e1(v, it=it, nc=nc, pmark=pmark, pw=pw):
                wait(v, "V", M[f"p3g_{it}_{nc}"], pmark, pw)
                M[f"e_s0_{it}_{nc}"] = inc(v.tensor_copy(g0, PB[nc % 2][:, :]), "v")
            blk.vector(p3_e1)

            def p3_e2(s, it=it, nc=nc, pmark=pmark, pw=pw):
                wait(s, "A", M[f"p3g_{it}_{nc}"])
                if pmark is not None:
                    wait(s, "A", pmark, pw)
                M[f"e_s1_{it}_{nc}"] = inc(s.activation(
                    gA, PB[nc % 2][:, :], AF.Square, scale=n32(SQ_A)), "a")
            blk.scalar(p3_e2)

            pt = M.get(f"e_t_{prev[0]}_{prev[1]}") if (it * 16 + nc) >= 1 else None

            def p3_e3(g, it=it, nc=nc, pmark=pmark, pt=pt):
                wait(g, "G", M[f"e_s0_{it}_{nc}"], M[f"e_s1_{it}_{nc}"])
                if pmark is not None:
                    wait(g, "G", pmark, pt)
                a = inc(g.tensor_mul(gB, gA, g0), "p")
                wait(g, "G", a)
                M[f"e_w_{it}_{nc}"] = inc(g.tensor_add(gA, gB, g0), "p")
            blk.gpsimd(p3_e3)

            def p3_e4(s, it=it, nc=nc):
                wait(s, "A", M[f"e_w_{it}_{nc}"])
                M[f"e_t_{it}_{nc}"] = inc(s.activation(
                    gB, gA, AF.Tanh, scale=n32(GELU_B)), "a")
            blk.scalar(p3_e4)

            def p3_e5(v, it=it, nc=nc, pmark=pmark):
                wait(v, "V", M[f"e_t_{it}_{nc}"], pmark)
                a = inc(v.scalar_tensor_tensor(gA, gB, 1.0, g0, ALU.add, ALU.mult), "v")
                b = inc(v.tensor_scalar(gC, PB[2 + nc % 2][:, :], 0.5, None, ALU.mult), "v")
                M[f"e_up_{it}_{nc}"] = b
                wait(v, "V", b)
                m = inc(v.tensor_mul(inter[:, nc * 512:(nc + 1) * 512], gA, gC), "v")
                M[f"p3e_{it}_{nc}"] = m
            blk.vector(p3_e5)

        for hc in range(16):
            def p3_dmm(t, it=it, hc=hc):
                if hc == 0:
                    wait(t, "T", M[f"p3e_{it}_15"])
                bank = PB[4 + hc % 2]
                if hc >= 2:
                    wait(t, "T", M[f"dev_{it}_{hc - 2}"])
                elif it >= 1:
                    wait(t, "T", M[f"dev_{it - 1}_{14 + hc}"])
                mm = None
                for nc in range(16):
                    mm = t.matmul(bank[:, :],
                                  dres[:, nc * 2048 + hc * 128: nc * 2048 + hc * 128 + 128],
                                  inter[:, nc * 512:(nc + 1) * 512],
                                  start=(nc == 0), stop=(nc == 15))
                M[f"p3d_{it}_{hc}"] = inc(mm, "t")
                if hc == 15:
                    M[f"p3mm_{it}"] = ("t", C.cnt["t"])
            blk.tensor(p3_dmm)

            def p3_dev(s, it=it, hc=hc):
                wait(s, "A", M[f"p3d_{it}_{hc}"])
                if hc >= 2:
                    wait(s, "A", M[f"dst_{it}_{hc - 2}"])
                elif it >= 1:
                    wait(s, "A", M[f"dst_{it - 1}_{14 + hc}"])
                M[f"dev_{it}_{hc}"] = inc(s.activation(
                    stgD(hc % 2), PB[4 + hc % 2][:, :], AF.Copy), "a")
            blk.scalar(p3_dev)

            def p3_dst(s, it=it, sl=sl, rq=rq, hc=hc):
                wait(s, "S", M[f"dev_{it}_{hc}"])
                dst = C.rs_i[sl].rearrange("(r c p) t -> p r c t", r=NCORES, p=128)[
                    :, 2 * rq:2 * rq + 2, hc, :]
                M[f"dst_{it}_{hc}"] = dmx(
                    s, "sE", dst, stgD(hc % 2).rearrange("p (r t) -> p r t", r=2))
                if hc == 15 and it + 2 < 16:
                    ring_load(s, it + 2)
            blk.sync(p3_dst)

        if rq == 3:
            def p3_rs(g, sl=sl, it=it):
                wait(g, "G", M[f"dst_{it}_15"])
                M[f"rs_{sl}"] = inc(g.collective_compute(
                    "ReduceScatter", ALU.add, replica_groups=[list(range(NCORES))],
                    ins=[C.rs_i[sl].opt()], outs=[C.rs_o[sl].opt()]), "cr")
            blk.gpsimd(p3_rs)

    M["p3_pe_end"] = ("t", C.cnt["t"])
    M["p3_a_end"] = ("a", C.cnt["a"])
    M["p3_v_end"] = ("v", C.cnt["v"])
    M["p3_p_end"] = ("p", C.cnt["p"])


def _p4(C):
    blk, vf, vb, PB, PBb = C.blk, C.vf, C.vb, C.PB, C.PBb
    inc, wait, M = C.inc, C.wait, C.M
    dmx = C.dmx

    kv = vf(RXN, 2048)
    kv_bf = vb(RXN + 2048, 2048)
    arg = vf(RXN + 3072, 2048)
    wj = vf(RXN + 5120, 2048)
    qscr = lambda par: vb(RXN + 7168 + par * 128, 256)
    qkd = lambda par, half: vb(RXN + 7424 + (par * 2 + half) * 128, 256)
    e_slot = lambda par: vf(RX + par * 2048, 2048)
    slbc = vf(RX + 12288, 2048)
    qb = vb(RX, 4096)
    kb = vb(RX + 2048, 4096)
    vtb = vb(RX + 4096, 4096)
    ub = vf(RX + 6144, 4096)
    cb = vf(RX + 10240, 2048)
    o_st = vf(RSQ, 4096)
    sq = vf(RSQ + 4096, 4096)
    qd_t = vb(RWS, 4096)
    dg_t = vb(RWS + 2048, 8192)
    bd_t = vb(RWS + 6144, 2048)
    wosl = lambda par: vb(RX + 14336 + par * 1024, 2048)
    s_stg = lambda par: vf(RXN + 5120 + par * 512, 512)
    ou = vb(RKD, 16384)

    def p4_pre_s(s):
        wait(s, "S", M["p3_pe_end"], M["p3_v_end"], M["p3_a_end"], ("c2", 1))
        M["slbc"] = dmx(s, "sD", slbc, C._slope_bc)
        for j in range(2):
            M[f"eld_{j}"] = dmx(s, "sD", e_slot(j)[:, :],
                                C.e_all[j * 128:(j + 1) * 128, :])
    blk.sync(p4_pre_s)

    def p4_dec_a(s):
        wait(s, "A", M["nslc"], M["p3_pe_end"], M["p3_v_end"], M["slbc"])
        for h in range(H):
            inc(s.activation(qd_t[:, h * BLK:(h + 1) * BLK], C.sb_arr[:, :],
                             AF.Exp, scale=C.sb_nslc[:, h:h + 1]), "a")
            for half in range(2):
                inc(s.activation(dg_t[:, h * 512 + half * BLK: h * 512 + (half + 1) * BLK],
                                 C.sb_mt[:, half * BLK:(half + 1) * BLK],
                                 AF.Exp, scale=C.sb_nslc[:, h:h + 1]), "a")
        inc(s.activation(bd_t, slbc, AF.Exp, scale=n32(-256.0)), "a")
        M["p4_dec"] = ("a", C.cnt["a"])
    blk.scalar(p4_dec_a)

    def p4_hz(v):
        wait(v, "V", M["p3_v_end"], M["p3_pe_end"])
        M["kv_z"] = inc(v.memset(kv, 0.0), "v")
    blk.vector(p4_hz)

    for j in range(NCORES):
        if j >= 2:
            def p4_eld(s, j=j):
                wait(s, "S", M[f"hstep_{j - 2}"])
                M[f"eld_{j}"] = dmx(s, "sD", e_slot(j % 2)[:, :],
                                    C.e_all[j * 128:(j + 1) * 128, :])
            blk.sync(p4_eld)

        def p4_harg(v, j=j):
            wait(v, "V", M["slbc"], M["kv_z"])
            if j >= 1:
                wait(v, "V", M[f"hstep_{j - 1}"])
            M[f"harg_{j}"] = inc(v.tensor_scalar(
                arg, slbc, C.sb_dexp[:, j:j + 1], None, ALU.mult), "v")
        blk.vector(p4_harg)

        def p4_hexp(s, j=j):
            wait(s, "A", M[f"harg_{j}"])
            if j >= 1:
                wait(s, "A", M[f"hstep_{j - 1}"])
            M[f"hexp_{j}"] = inc(s.activation(wj, arg, AF.Exp, scale=n32(-1.0)), "a")
        blk.scalar(p4_hexp)

        def p4_hacc(v, j=j):
            wait(v, "V", M[f"hexp_{j}"], M[f"eld_{j}"])
            a = inc(v.tensor_mul(arg, e_slot(j % 2), wj), "v")
            M[f"hw_{j}"] = a
            wait(v, "V", a)
            M[f"hstep_{j}"] = inc(v.tensor_add(kv, kv, arg), "v")
        blk.vector(p4_hacc)

    for b in range(BPC):
        def p4_bld(s, b=b):
            if b == 0:
                wait(s, "S", M[f"hstep_{NCORES - 1}"], M["p4_dec"])
            else:
                wait(s, "S", M[f"blk_done_{b - 1}"], M[f"omm_{b - 1}_15"],
                     M[f"supd_{b - 1}"])
            dmx(s, "sB", qb, C.q_buf[b])
            dmx(s, "sB", kb, C.k_buf[b])
            dmx(s, "sB", vtb, C.vt_buf[b])
            dmx(s, "sB", ub, C.u_buf[b])
            dmx(s, "sB", cb, C.c_bufd[b])
            M[f"bld_{b}"] = ("sB", C.cnt["sB"])
        blk.sync(p4_bld)

        def p4_state(v, b=b):
            wait(v, "V", M["p4_dec"])
            if b == 0:
                wait(v, "V", M[f"hstep_{NCORES - 1}"])
            if b > 0:
                wait(v, "V", M[f"supd_{b - 1}"])
            if b > 0:
                wait(v, "V", M[f"omm_{b - 1}_15"])
            M[f"kvbf_{b}"] = inc(v.tensor_copy(kv_bf, kv), "v")
        blk.vector(p4_state)

        for h in range(H):
            def p4_qk(t, b=b, h=h):
                wait(t, "T", M[f"bld_{b}"])
                if h >= 2:
                    wait(t, "T", M[f"qkd_{b}_{h - 2}"])
                elif b >= 1:
                    wait(t, "T", M[f"qkd_{b - 1}_{14 + h}"])
                mm = None
                for half in range(2):
                    mm = t.matmul(PB[h % 2][:, half * BLK:(half + 1) * BLK],
                                  kb[:, h * BLK + half * 128: h * BLK + half * 128 + 128],
                                  qb[:, h * BLK:(h + 1) * BLK],
                                  start=True, stop=True)
                M[f"qk_{b}_{h}"] = inc(mm, "t")
            blk.tensor(p4_qk)

            def p4_vdec(v, b=b, h=h):
                wait(v, "V", M[f"qk_{b}_{h}"], M["p4_dec"])
                if h >= 2:
                    wait(v, "V", M[f"omm_{b}_{h - 2}"])
                elif b >= 1:
                    wait(v, "V", M[f"omm_{b - 1}_{14 + h}"])
                inc(v.tensor_mul(qscr(h % 2), qb[:, h * BLK:(h + 1) * BLK],
                                 qd_t[:, h * BLK:(h + 1) * BLK]), "v")
                m = None
                for half in range(2):
                    m = inc(v.tensor_mul(
                        qkd(h % 2, half), PB[h % 2][:, half * BLK:(half + 1) * BLK],
                        dg_t[:, h * 512 + half * BLK: h * 512 + (half + 1) * BLK]), "v")
                M[f"qkd_{b}_{h}"] = m
            blk.vector(p4_vdec)

            def p4_omm(t, b=b, h=h):
                wait(t, "T", M[f"qkd_{b}_{h}"], M[f"kvbf_{b}"])
                if h >= 2:
                    wait(t, "T", M[f"oev_{b}_{h - 2}"])
                elif b >= 1:
                    wait(t, "T", M[f"oev_{b - 1}_{14 + h}"])
                for half in range(2):
                    t.matmul(PB[2 + h % 2][:, 0:BLK],
                             vtb[:, h * BLK + half * 128: h * BLK + half * 128 + 128],
                             qkd(h % 2, half),
                             start=(half == 0), stop=False)
                mm = t.matmul(PB[2 + h % 2][:, 0:BLK], kv_bf[:, h * 128:(h + 1) * 128],
                              qscr(h % 2), start=False, stop=True)
                M[f"omm_{b}_{h}"] = inc(mm, "t")
            blk.tensor(p4_omm)

            def p4_oev(s, b=b, h=h):
                wait(s, "A", M[f"omm_{b}_{h}"])
                if b >= 1:
                    wait(s, "A", M[f"ou_{b - 1}"])
                M[f"oev_{b}_{h}"] = inc(s.activation(
                    o_st[:, h * BLK:(h + 1) * BLK], PB[2 + h % 2][:, 0:BLK],
                    AF.Copy), "a")
            blk.scalar(p4_oev)

        def p4_supd(v, b=b):
            wait(v, "V", M[f"kvbf_{b}"], M[f"bld_{b}"], M["p4_dec"])
            a = inc(v.tensor_mul(kv, kv, bd_t), "v")
            wait(v, "V", a)
            M[f"supd_{b}"] = inc(v.tensor_add(kv, kv, cb), "v")
        blk.vector(p4_supd)

        def p4_sq(s, b=b):
            wait(s, "A", M[f"oev_{b}_15"])
            if b >= 1:
                wait(s, "A", M[f"ou_{b - 1}"])
            M[f"sq_{b}"] = inc(s.activation(sq, o_st, AF.Square), "a")
        blk.scalar(p4_sq)

        def p4_rmm(t, b=b):
            wait(t, "T", M[f"sq_{b}"], M["ones"])
            if b >= 1:
                wait(t, "T", M[f"rstd_{b - 1}"])
            mm = None
            for h in range(H):
                mm = t.matmul(PB[4][0:1, 0:BLK], C.sb_onec[:, :],
                              sq[:, h * BLK:(h + 1) * BLK],
                              start=(h == 0), stop=(h == H - 1))
            M[f"rmm_{b}"] = inc(mm, "t")
        blk.tensor(p4_rmm)

        def p4_rstd_a(s, b=b):
            wait(s, "A", M[f"rmm_{b}"])
            M[f"rstd_a_{b}"] = inc(s.activation(
                C.sb_st[0:1, 0:BLK], PB[4][0:1, 0:BLK], AF.Sqrt,
                bias=n32(RMS_EPS), scale=n32(1.0 / D)), "a")
        blk.scalar(p4_rstd_a)

        def p4_rstd_v(v, b=b):
            wait(v, "V", M[f"rstd_a_{b}"])
            M[f"rstd_{b}"] = inc(v.reciprocal(C.sb_st[0:1, BLK:2 * BLK],
                                              C.sb_st[0:1, 0:BLK]), "v")
        blk.vector(p4_rstd_v)

        def p4_rbc(t, b=b):
            wait(t, "T", M[f"rstd_{b}"])
            if b >= 1:
                wait(t, "T", M[f"ou_{b - 1}"])
            M[f"rbc_{b}"] = inc(t.matmul(PB[5][:, 0:BLK], C.sb_oner[0:1, :],
                                         C.sb_st[0:1, BLK:2 * BLK],
                                         start=True, stop=True), "t")
        blk.tensor(p4_rbc)

        def p4_ou(v, b=b):
            wait(v, "V", M[f"rbc_{b}"], M[f"bld_{b}"])
            if b == 0:
                wait(v, "V", M["dst_15_15"])
            m = None
            for h in range(H):
                a = inc(v.tensor_mul(sq[:, h * BLK:(h + 1) * BLK],
                                     o_st[:, h * BLK:(h + 1) * BLK],
                                     PB[5][:, 0:BLK]), "v")
                wait(v, "V", a)
                m = inc(v.tensor_mul(
                    ou[:, h * 1024 + b * BLK: h * 1024 + b * BLK + BLK],
                    sq[:, h * BLK:(h + 1) * BLK], ub[:, h * BLK:(h + 1) * BLK]), "v")
            M[f"ou_{b}"] = m
            M[f"blk_done_{b}"] = m
        blk.vector(p4_ou)

    def p4_wo0(s):
        wait(s, "S", M["p3_pe_end"], M["p4_dec"])
        for oc in range(2):
            M[f"wold_{oc}"] = dmx(s, "sA", wosl(oc % 2),
                                  C.wo_bf[oc].rearrange("p kc j -> p (kc j)"))
    blk.sync(p4_wo0)

    for oc in range(16):
        def p4_womm(t, oc=oc):
            wait(t, "T", M[f"wold_{oc}"], M["blk_done_3"])
            for tt in range(2):
                bank = PB[6 + tt]
                if oc >= 1:
                    wait(t, "T", M[f"sev_{oc - 1}_{tt}"])
                mm = None
                for kc in range(16):
                    mm = t.matmul(bank[:, :],
                                  wosl(oc % 2)[:, kc * 128:(kc + 1) * 128],
                                  ou[:, kc * 1024 + tt * 512: kc * 1024 + tt * 512 + 512],
                                  start=(kc == 0), stop=(kc == 15))
                M[f"womm_{oc}_{tt}"] = inc(mm, "t")
            M[f"womm_{oc}"] = ("t", C.cnt["t"])
        blk.tensor(p4_womm)

        if oc + 2 < 16:
            def p4_wold2(s, oc=oc):
                wait(s, "S", M[f"womm_{oc}"])
                M[f"wold_{oc + 2}"] = dmx(
                    s, "sA", wosl((oc + 2) % 2),
                    C.wo_bf[oc + 2].rearrange("p kc j -> p (kc j)"))
            blk.sync(p4_wold2)

        def p4_sev(s, oc=oc):
            for tt in range(2):
                wait(s, "A", M[f"womm_{oc}_{tt}"])
                if oc >= 1:
                    wait(s, "A", M[f"sst_{oc - 1}_{tt}"])
                M[f"sev_{oc}_{tt}"] = inc(s.activation(
                    s_stg(tt), PB[6 + tt][:, :], AF.Copy), "a")
        blk.scalar(p4_sev)

        def p4_sst(s, oc=oc):
            for tt in range(2):
                wait(s, "S", M[f"sev_{oc}_{tt}"])
                M[f"sst_{oc}_{tt}"] = dmx(
                    s, "sF",
                    C.s_bufd[:, oc * 1024 + tt * 512: oc * 1024 + tt * 512 + 512],
                    s_stg(tt))
        blk.sync(p4_sst)

    M["p4_pe_end"] = ("t", C.cnt["t"])
    M["p4_v_end"] = ("v", C.cnt["v"])
    M["p4_a_end"] = ("a", C.cnt["a"])


def _p5(C):
    blk, vf, vb, inc, wait, M = C.blk, C.vf, C.vb, C.inc, C.wait, C.M
    dmx = C.dmx
    gsl = vf(RX, 4096)
    xsl = vf(RX + 4096, 4096)
    ssl = vf(RX + 8192, 4096)
    osl = vf(RX + 12288, 4096)

    for sl in range(BPC):
        def p5_ld(s, sl=sl):
            wait(s, "S", ("cr", sl + 1), M["sst_15_1"], M["p4_pe_end"],
                 M["p4_v_end"], M["p4_a_end"])
            if sl >= 1:
                wait(s, "S", M[f"ost_{sl - 1}"])
            dmx(s, "sB", gsl.rearrange("p (c t) -> p c t", c=16),
                C.rs_o[sl].rearrange("(c p) t -> p c t", p=128))
            dmx(s, "sB", xsl.rearrange("p (c t) -> p c t", c=16),
                C.x_dev.rearrange("(c p) t -> p c t", p=128)[
                    :, :, sl * BLK:(sl + 1) * BLK])
            dmx(s, "sB", ssl.rearrange("p (c t) -> p c t", c=16),
                C.s_bufd.rearrange("p (c t) -> p c t", c=16)[
                    :, :, sl * BLK:(sl + 1) * BLK])
            M[f"p5ld_{sl}"] = ("sB", C.cnt["sB"])
        blk.sync(p5_ld)

        def p5_silu(s, sl=sl):
            wait(s, "A", M[f"p5ld_{sl}"])
            if sl >= 1:
                wait(s, "A", M[f"p5v_{sl - 1}"], M[f"ost_{sl - 1}"])
            M[f"p5a_{sl}"] = inc(s.activation(osl, gsl, AF.Silu), "a")
        blk.scalar(p5_silu)

        def p5_v(v, sl=sl):
            wait(v, "V", M[f"p5a_{sl}"])
            a = inc(v.tensor_mul(gsl, osl, ssl), "v")
            wait(v, "V", a)
            M[f"p5v_{sl}"] = inc(v.tensor_add(osl, gsl, xsl), "v")
        blk.vector(p5_v)

        def p5_st(s, sl=sl):
            wait(s, "S", M[f"p5v_{sl}"])
            M[f"ost_{sl}"] = dmx(
                s, "sE",
                C.o_out.rearrange("(c p) t -> p c t", p=128)[
                    :, :, sl * BLK:(sl + 1) * BLK],
                osl.rearrange("p (c t) -> p c t", c=16))
        blk.sync(p5_st)


def build(phases=5, debug_outs=()):
    nc = bass.Bass("TRN2", target_bir_lowering=False)

    x_dev = nc.dram_tensor("x_dev", [D, TPC], F32, kind="ExternalInput").ap()
    wq_t = nc.dram_tensor("wq_t", [16, 128, 16, 512], F32, kind="ExternalInput").ap()
    wo_t = nc.dram_tensor("wo_t", [16, 128, 16, 128], F32, kind="ExternalInput").ap()
    slope_col = nc.dram_tensor("slope_col", [128, H], F32, kind="ExternalInput").ap()
    slope_bcd = nc.dram_tensor("slope_bc", [128, D], F32, kind="ExternalInput").ap()
    dexp_in = nc.dram_tensor("dexp", [128, 8], F32, kind="ExternalInput").ap()
    pk_in = {m: nc.dram_tensor(f"{m}_pk", [1024, 2048], I32, kind="ExternalInput").ap()
             for m in ("gate", "up", "down")}
    sc_in = {m: nc.dram_tensor(f"{m}_sc", [1024, 2048], F32, kind="ExternalInput").ap()
             for m in ("gate", "up", "down")}

    o_out = nc.dram_tensor("o_out", [D, TPC], F32, kind="ExternalOutput").ap()

    ident_c = nc.inline_tensor(np.eye(128, dtype=f32).astype(bf), "identc")
    arr_c = nc.inline_tensor(np.ascontiguousarray(
        np.broadcast_to(np.arange(1, BLK + 1, dtype=f32), (128, BLK)).astype(bf)), "arrc")
    rev_c = nc.inline_tensor(np.ascontiguousarray(
        np.broadcast_to(BLK - np.arange(1, BLK + 1, dtype=f32), (128, BLK)).astype(bf)), "revc")
    i_idx = np.arange(1, BLK + 1, dtype=f32)
    mtm = i_idx[None, :] - i_idx[:, None]
    mtm = np.where(mtm >= 0, mtm, GBIG).astype(f32)
    mt_c = nc.inline_tensor(mtm.astype(bf), "mtc")

    xn_loc = nc.dram_tensor("xn_loc", [D, TPC], BF16, kind="Internal").ap()
    xn_full = nc.dram_tensor("xn_full", [NCORES * D, TPC], BF16, kind="Internal").ap()
    q_buf = nc.dram_tensor("q_buf", [BPC, 128, H * BLK], BF16, kind="Internal").ap()
    k_buf = nc.dram_tensor("k_buf", [BPC, 128, H * BLK], BF16, kind="Internal").ap()
    vt_buf = nc.dram_tensor("vt_buf", [BPC, 128, H * BLK], BF16, kind="Internal").ap()
    u_buf = nc.dram_tensor("u_buf", [BPC, 128, H * BLK], F32, kind="Internal").ap()
    c_bufd = nc.dram_tensor("c_buf", [BPC, 128, D], F32, kind="Internal").ap()
    e_loc = nc.dram_tensor("e_loc", [128, D], F32, kind="Internal").ap()
    e_all = nc.dram_tensor("e_all", [NCORES * 128, D], F32, kind="Internal").ap()
    wq_bf = nc.dram_tensor("wq_bf", [16, 128, 16, 512], BF16, kind="Internal").ap()
    wo_bf = nc.dram_tensor("wo_bf", [16, 128, 16, 128], BF16, kind="Internal").ap()
    dec_g = nc.dram_tensor("dec_g", [D, MLPC], BF16, kind="Internal").ap()
    dec_d = nc.dram_tensor("dec_d", [MLPC, D], BF16, kind="Internal").ap()
    dec_u = nc.dram_tensor("dec_u", [16, 128, 2048], BF16, kind="Internal").ap()
    s_bufd = nc.dram_tensor("s_bufd", [128, H * TPC], F32, kind="Internal").ap()
    rs_i = [nc.dram_tensor(f"rs_in{s}", [NCORES * D, BLK], F32, kind="Internal").ap()
            for s in range(BPC)]
    rs_o = [nc.dram_tensor(f"rs_out{s}", [D, BLK], F32, kind="Internal").ap()
            for s in range(BPC)]

    dbg = {}
    DBG_SPECS = {"xn": xn_loc, "xnf": xn_full, "q": q_buf, "k": k_buf, "vt": vt_buf,
                 "u": u_buf, "cb": c_bufd, "eall": e_all, "dg": dec_g, "du": dec_u,
                 "dd": dec_d, "g0": rs_o[0], "g1": rs_o[1], "g2": rs_o[2],
                 "g3": rs_o[3], "sb": s_bufd}
    for name in debug_outs:
        src = DBG_SPECS[name]
        dbg[name] = (nc.dram_tensor("dbg_" + name, list(src.shape), src.dtype,
                                    kind="ExternalOutput").ap(), src)

    from contextlib import ExitStack
    with ExitStack() as ctx:
        def sbuf(name, shape, dt):
            return ctx.enter_context(nc.sbuf_tensor(name, shape, dt))

        def sem(name):
            return ctx.enter_context(nc.semaphore(name))

        sb_id = sbuf("sb_id", [128, 128], BF16)
        sb_arr = sbuf("sb_arr", [128, BLK], BF16)
        sb_rev = sbuf("sb_rev", [128, BLK], BF16)
        sb_mt = sbuf("sb_mt", [128, 2 * BLK], BF16)
        sb_slc = sbuf("sb_slc", [128, H], F32)
        sb_nslc = sbuf("sb_nslc", [128, H], F32)
        sb_bdc = sbuf("sb_bdc", [128, H], F32)
        sb_dexp = sbuf("sb_dexp", [128, 8], F32)
        sb_onec = sbuf("sb_onec", [128, 1], F32)
        sb_oner = sbuf("sb_oner", [1, 128], F32)
        sb_st = sbuf("sb_st", [1, 2048], F32)
        AR = sbuf("AR", [128, 49152], F32)
        ARf = AR.ap()
        ARb = ARf.bitcast(BF16)
        ARi = ARf.bitcast(I32)

        def vf(off, ncols):
            return ARf[:, off:off + ncols]

        def vb(off_f32, ncols_bf):
            return ARb[:, 2 * off_f32: 2 * off_f32 + ncols_bf]

        def vi(off, ncols):
            return ARi[:, off:off + ncols]

        PB = [ctx.enter_context(nc.psum_tensor(f"PB{i}", [128, 512], F32))
              for i in range(8)]
        PBb = [p.ap().bitcast(BF16) for p in PB]

        gd = sem("gd"); sv = sem("sv"); sa = sem("sa")
        st_ = sem("st"); pl = sem("pl")
        cc1 = sem("cc1"); cc2 = sem("cc2"); ccr = sem("ccr")
        SEMS = {"gd": gd, "v": sv, "a": sa, "t": st_, "p": pl,
                "c1": cc1, "c2": cc2, "cr": ccr}
        for _k in ("sA", "sB", "sC", "sD", "sE", "sF", "sG", "sH"):
            SEMS[_k] = sem(_k)
        cnt = {k: 0 for k in SEMS}
        wm = {}

        def inc(ins, key, n=1):
            ins.then_inc(SEMS[key], n)
            cnt[key] += n
            return (key, cnt[key])

        def dmx(q, key, out, in_):
            q.wait_ge(SEMS[key], cnt[key])
            ins = q.dma_start(out=out, in_=in_)
            ins.then_inc(SEMS[key], 16)
            cnt[key] += 16
            return (key, cnt[key])

        def wait(eng, ename, *deps):
            w = wm.setdefault(ename, {})
            for dep in deps:
                if dep is None:
                    continue
                key, val = dep
                if val > w.get(key, 0):
                    eng.wait_ge(SEMS[key], val)
                    w[key] = val

        M = {}

        with nc.Block() as blk:
            def p0_sync(s):
                for dst, src in [
                    (sb_id[:, :], ident_c.ap()), (sb_arr[:, :], arr_c.ap()),
                    (sb_rev[:, :], rev_c.ap()),
                    (sb_mt[:, 0:BLK], mt_c.ap()[0:128, :]),
                    (sb_mt[:, BLK:2 * BLK], mt_c.ap()[128:256, :]),
                    (sb_slc[:, :], slope_col), (sb_dexp[:, :], dexp_in),
                ]:
                    dmx(s, "sH", dst, src)
                dmx(s, "sH",
                    vf(RX, 16 * TPC).rearrange("p (c t) -> p c t", c=NCH),
                    x_dev.rearrange("(c p) t -> p c t", p=128))
                M["p0"] = ("sH", cnt["sH"])
            blk.sync(p0_sync)

            def p0_g(g):
                dmx(g, "gd", wq_bf, wq_t)
                M["wcast"] = dmx(g, "gd", wo_bf, wo_t)
            blk.gpsimd(p0_g)

            def p0_v(v):
                inc(v.memset(sb_onec[:, :], 1.0), "v")
                M["ones"] = inc(v.memset(sb_oner[:, :], 1.0), "v")
            blk.vector(p0_v)

            def p0_a(s):
                wait(s, "a", M["p0"])
                M["nslc"] = inc(s.mul(sb_nslc[:, :], sb_slc[:, :], -1.0), "a")
                M["bdc"] = inc(s.activation(sb_bdc[:, :], sb_slc[:, :], AF.Exp,
                                            scale=n32(-256.0)), "a")
            blk.scalar(p0_a)

            for tt in range(2):
                def p1_sq(s, tt=tt):
                    wait(s, "a", M["p0"])
                    if tt == 1:
                        wait(s, "a", M["ln_mm_0"], M["xn_v_0"])
                    for c in range(NCH):
                        M[f"sq_{tt}_{c}"] = inc(s.activation(
                            vf(RSQ + c * 512, 512),
                            vf(RX + c * TPC + tt * 512, 512), AF.Square), "a")
                blk.scalar(p1_sq)

                def p1_mm(t, tt=tt):
                    wait(t, "t", M["p0"], M["ones"])
                    if tt == 1:
                        wait(t, "t", M["ln_rd_0"])
                    for c in range(NCH):
                        t.matmul(PB[0][0:1, :], sb_onec[:, :],
                                 vf(RX + c * TPC + tt * 512, 512),
                                 start=(c == 0), stop=(c == NCH - 1))
                    for c in range(NCH):
                        wait(t, "t", M[f"sq_{tt}_{c}"])
                        mm = t.matmul(PB[1][0:1, :], sb_onec[:, :],
                                      vf(RSQ + c * 512, 512),
                                      start=(c == 0), stop=(c == NCH - 1))
                    M[f"ln_mm_{tt}"] = inc(mm, "t")
                blk.tensor(p1_mm)

                def p1_rv(v, tt=tt):
                    wait(v, "v", M[f"ln_mm_{tt}"])
                    mean, e1, ng, var = (sb_st[0:1, 0:512], sb_st[0:1, 512:1024],
                                         sb_st[0:1, 1024:1536], sb_st[0:1, 1536:2048])
                    a = inc(v.tensor_scalar(mean, PB[0][0:1, :], 1.0 / D, None, ALU.mult), "v")
                    b = inc(v.tensor_scalar(e1, PB[1][0:1, :], 1.0 / D, None, ALU.mult), "v")
                    wait(v, "v", b)
                    c = inc(v.scalar_tensor_tensor(ng, mean, -1.0, mean, ALU.mult, ALU.mult), "v")
                    wait(v, "v", c)
                    M[f"ln_var_{tt}"] = inc(v.tensor_add(var, e1, ng), "v")
                blk.vector(p1_rv)

                def p1_ra(s, tt=tt):
                    wait(s, "a", M[f"ln_var_{tt}"])
                    M[f"ln_std_{tt}"] = inc(s.activation(
                        sb_st[0:1, 1024:1536], sb_st[0:1, 1536:2048], AF.Sqrt,
                        bias=n32(LN_EPS)), "a")
                blk.scalar(p1_ra)

                def p1_rv2(v, tt=tt):
                    wait(v, "v", M[f"ln_std_{tt}"])
                    rstd = sb_st[0:1, 1536:2048]
                    a = inc(v.reciprocal(rstd, sb_st[0:1, 1024:1536]), "v")
                    wait(v, "v", a)
                    M[f"ln_rd_{tt}"] = inc(v.tensor_mul(
                        sb_st[0:1, 1024:1536], sb_st[0:1, 0:512], rstd), "v")
                blk.vector(p1_rv2)

                def p1_bc(t, tt=tt):
                    wait(t, "t", M[f"ln_rd_{tt}"])
                    if tt == 1:
                        wait(t, "t", M["xn_v_0"])
                    t.matmul(PB[2][:, :], sb_oner[0:1, :], sb_st[0:1, 1536:2048],
                             start=True, stop=True)
                    M[f"ln_bc_{tt}"] = inc(t.matmul(
                        PB[3][:, :], sb_oner[0:1, :], sb_st[0:1, 1024:1536],
                        start=True, stop=True), "t")
                blk.tensor(p1_bc)

                def p1_xn(v, tt=tt):
                    wait(v, "v", M[f"ln_bc_{tt}"])
                    for c in range(NCH):
                        a = inc(v.tensor_mul(vf(RSQ + c * 512, 512),
                                             vf(RX + c * TPC + tt * 512, 512),
                                             PB[2][:, :]), "v")
                        wait(v, "v", a)
                        m = inc(v.tensor_sub(
                            vb(RXN, 16384)[:, c * 1024 + tt * 512: c * 1024 + tt * 512 + 512],
                            vf(RSQ + c * 512, 512), PB[3][:, :]), "v")
                    M[f"xn_v_{tt}"] = m
                blk.vector(p1_xn)

            def p1_st(s):
                wait(s, "S", M["xn_v_1"])
                M["xn_st"] = dmx(
                    s, "sG", xn_loc.rearrange("(c p) t -> p c t", p=128),
                    vb(RXN, 16384).rearrange("p (c t) -> p c t", c=NCH))
            blk.sync(p1_st)

            def p1_ag(g):
                wait(g, "gd", M["xn_st"])
                M["ag_xn"] = inc(g.collective_compute(
                    "AllGather", ALU.bypass, replica_groups=[list(range(NCORES))],
                    ins=[xn_loc.opt()], outs=[xn_full.opt()]), "c1")
            blk.gpsimd(p1_ag)

            from types import SimpleNamespace
            C = SimpleNamespace(
                nc=nc, blk=blk, vf=vf, vb=vb, vi=vi, PB=PB, PBb=PBb,
                inc=inc, wait=wait, M=M, cnt=cnt, SEMS=SEMS, dmx=dmx,
                sb_id=sb_id, sb_arr=sb_arr, sb_rev=sb_rev, sb_mt=sb_mt,
                sb_slc=sb_slc, sb_nslc=sb_nslc, sb_bdc=sb_bdc, sb_dexp=sb_dexp,
                sb_onec=sb_onec, sb_oner=sb_oner, sb_st=sb_st,
                xn_full=xn_full, q_buf=q_buf, k_buf=k_buf, vt_buf=vt_buf,
                u_buf=u_buf, c_bufd=c_bufd, e_loc=e_loc, e_all=e_all,
                wq_bf=wq_bf, wo_bf=wo_bf, dec_g=dec_g, dec_d=dec_d, dec_u=dec_u,
                s_bufd=s_bufd, rs_i=rs_i, rs_o=rs_o, o_out=o_out, x_dev=x_dev,
                _slope_bc=slope_bcd,
                pk_in=pk_in, sc_in=sc_in, phases=phases,
            )
            if phases >= 2:
                _p2(C)
            if phases >= 3:
                _p3(C)
            if phases >= 4:
                _p4(C)
            if phases >= 5:
                _p5(C)

            def fin_sync(s):
                for key in ("sA", "sB", "sC", "sD", "sE", "sF", "sG", "sH"):
                    if cnt[key] > 0:
                        s.wait_ge(SEMS[key], cnt[key])
            blk.sync(fin_sync)

            def fin_g(g):
                deps = [(k, cnt[k]) for k in ("sA", "sB", "sC", "sD", "sE", "sF",
                                              "sG", "sH", "v", "a", "t", "p",
                                              "c1", "c2", "cr")]
                wait(g, "G", *[d for d in deps if d[1] > 0])
                for name, (dst, src_) in dbg.items():
                    dmx(g, "gd", dst, src_)
                g.wait_ge(gd, cnt["gd"])
            blk.gpsimd(fin_g)

    return nc, dbg




def prep_inputs(L0_attn_x, L0_attn_slope_rate, L0_attn_W_qkvu, L0_attn_W_out,
                L0_ffn_gate_W_packed, L0_ffn_gate_scales, L0_ffn_up_W_packed,
                L0_ffn_up_scales, L0_ffn_down_W_packed, L0_ffn_down_scales):
    x = np.asarray(L0_attn_x)[0]
    xd = np.ascontiguousarray(x[:, F_OF_PHI].T)
    wq = np.asarray(L0_attn_W_qkvu)
    wqT = wq[:, F_OF_PHI].T
    wq_host = np.ascontiguousarray(
        wqT.reshape(16, 128, 16, 512).transpose(2, 1, 0, 3))
    wo = np.asarray(L0_attn_W_out)
    woT = wo.T[:, F_OF_PHI]
    wo_host = np.ascontiguousarray(
        woT.reshape(16, 128, 16, 128).transpose(2, 1, 0, 3))
    slope = np.asarray(L0_attn_slope_rate).reshape(H).astype(np.float32)
    slope_colh = np.ascontiguousarray(np.broadcast_to(slope[None, :], (128, H)))
    slope_bch = np.ascontiguousarray(
        np.broadcast_to(np.repeat(slope, 128)[None, :], (128, D)))

    G = np.asarray(L0_ffn_gate_W_packed)
    U = np.asarray(L0_ffn_up_W_packed)
    Dn = np.asarray(L0_ffn_down_W_packed)
    Gs = np.asarray(L0_ffn_gate_scales)
    Us = np.asarray(L0_ffn_up_scales)
    Ds = np.asarray(L0_ffn_down_scales)
    DnP = Dn[F_OF_PHI]
    DsP = Ds[F_OF_PHI]

    in_maps = []
    for i in range(NCORES):
        sel = PI16[i * MLPC:(i + 1) * MLPC]
        dexp = np.array([1024.0 * (i - 1 - j) if j < i else 1e6 for j in range(8)],
                        dtype=np.float32)
        in_maps.append({
            "x_dev": np.ascontiguousarray(xd[:, i * TPC:(i + 1) * TPC]),
            "wq_t": wq_host, "wo_t": wo_host,
            "slope_col": slope_colh, "slope_bc": slope_bch,
            "dexp": np.ascontiguousarray(np.broadcast_to(dexp[None, :], (128, 8))),
            "gate_pk": np.ascontiguousarray(G[sel].T),
            "up_pk": np.ascontiguousarray(U[sel].T),
            "down_pk": np.ascontiguousarray(DnP.T[i * 1024:(i + 1) * 1024]),
            "gate_sc": np.ascontiguousarray(np.repeat(Gs[sel].T, 16, axis=0)).astype(np.float32),
            "up_sc": np.ascontiguousarray(np.repeat(Us[sel].T, 16, axis=0)).astype(np.float32),
            "down_sc": np.ascontiguousarray(
                np.repeat(DsP.T[64 * i:64 * (i + 1)], 16, axis=0)).astype(np.float32),
        })
    return in_maps


def assemble_output(results):
    parts = [np.asarray(r["o_out"]) for r in results]
    od = np.concatenate(parts, axis=1).T
    out = np.empty_like(od)
    out[:, F_OF_PHI] = od
    return out.reshape(1, TOK, D)


def kernel(**inputs):
    from concourse.bass_utils import run_bass_kernel_spmd
    in_maps = prep_inputs(**inputs)
    nc, _ = build(phases=5, debug_outs=())
    res = run_bass_kernel_spmd(nc, in_maps, core_ids=list(range(NCORES)))
    return assemble_output(res.results)
